# version 05 (final)
# baseline (speedup 1.0000x reference)
import math
import os
import numpy as np
import ml_dtypes
import concourse.bass as bass
import concourse.mybir as mybir
from concourse.bass_utils import run_bass_kernel_spmd

F32 = mybir.dt.float32
BF16 = mybir.dt.bfloat16
I32 = mybir.dt.int32
AF = mybir.ActivationFunctionType
ALU = mybir.AluOpType
AX = mybir.AxisListType

D = 1024
T = 2048
NT = T // 128
SCALE = 1.0 / math.sqrt(96.0)
EPS_LN = 1e-5
EPS_FFN = 1e-8
NEGBIG = -1e30
C32 = float(3 * 2 ** 22)
C64 = float(3 * 2 ** 51)
VS = 66
DC = 128
DR = 32

ENGINES = ("sync", "gpsimd", "tensor", "vector", "scalar")


class Em:
    def __init__(self, nc):
        self.nc = nc
        self.q = {e: [] for e in ENGINES}
        self.sem = {}
        self.cnt = {}
        self.waited = {e: {} for e in ENGINES}

    def sem_alloc(self, name):
        self.sem[name] = self.nc.alloc_semaphore(name)
        self.cnt[name] = 0

    def _prune(self, eng, wait):
        seen = {}
        for item in wait:
            if item is None:
                continue
            s, v = item
            if v is None:
                continue
            if v > seen.get(s, -1):
                seen[s] = v
        w = []
        for s, v in seen.items():
            if v > self.waited[eng].get(s, -1):
                self.waited[eng][s] = v
                w.append((s, v))
        return tuple(w)

    def emit(self, eng, fn, wait=(), inc=()):
        wait = self._prune(eng, wait)
        for s, amt in inc:
            self.cnt[s] += amt
        sems = self.sem

        def op(engine, fn=fn, wait=wait, inc=tuple(inc)):
            for s, v in wait:
                engine.wait_ge(sems[s], v)
            inst = fn(engine)
            for s, amt in inc:
                inst.then_inc(sems[s], amt)
        self.q[eng].append(op)

    def wait_only(self, eng, wait):
        wait = self._prune(eng, wait)
        sems = self.sem

        def op(engine, wait=wait):
            for s, v in wait:
                engine.wait_ge(sems[s], v)
        self.q[eng].append(op)

    def now(self, s):
        return self.cnt[s]


class Scope:

    def __init__(self, nc):
        self.nc = nc
        self.stack = []

    def alloc(self, name, shape, dt=F32):
        g = self.nc.sbuf_tensor(name, shape, dt)
        t = g.__enter__()
        self.stack[-1].append(g)
        return t.ap()

    def push(self):
        self.stack.append([])

    def pop(self):
        for g in reversed(self.stack.pop()):
            g.__exit__(None, None, None)


def _consts():
    c = {}
    c["ident"] = np.eye(128, dtype=np.float32).astype(ml_dtypes.bfloat16)
    c["negI"] = (np.eye(128) * NEGBIG).astype(np.float32).astype(ml_dtypes.bfloat16)
    c["U0"] = np.tril(np.ones((128, 128), np.float32), k=-1).astype(ml_dtypes.bfloat16)
    inv_freq = 1.0 / (10000.0 ** (np.arange(0, DR, 2, dtype=np.float64) / DR))
    t = np.arange(T, dtype=np.float64)
    freqs = np.outer(t, inv_freq)
    cos = np.cos(freqs).T.astype(np.float32)
    sin = np.sin(freqs).T.astype(np.float32)
    c["CS1c"] = np.concatenate([cos, cos], 0)
    c["CS2c"] = np.concatenate([-sin, sin], 0)
    c["onesrow"] = np.ones((1, 128), np.float32)
    sel = np.zeros((128, 64), np.float32)
    for j in range(8):
        sel[:, j * 8 + j] = 1.0
    c["selcols"] = sel.astype(ml_dtypes.bfloat16)
    c["ones32"] = np.ones((32, 8), np.float32).astype(ml_dtypes.bfloat16)
    return c


class _Cut(Exception):
    pass


def build_program(debug=(), cut=None):
    SKIP = os.environ.get("KSKIP", "")
    nc = bass.Bass("TRN2", target_bir_lowering=False, detect_race_conditions=False)
    import contextlib
    _cleanup = contextlib.ExitStack()
    _cleanup.enter_context(nc.cleanup_on_exit())
    em = Em(nc)
    sc = Scope(nc)

    def cutpoint(name):
        if cut == name:
            raise _Cut()

    dr = {}
    dr["x"] = nc.dram_tensor("x", [T, D], F32, kind="ExternalInput").ap()
    dr["c"] = nc.dram_tensor("c", [D], F32, kind="ExternalInput").ap()
    dr["wmod"] = nc.dram_tensor("wmod", [D, 6 * D], F32, kind="ExternalInput").ap()
    dr["W_kv_down"] = nc.dram_tensor("W_kv_down", [D, DC], F32, kind="ExternalInput").ap()
    dr["W_k_up"] = nc.dram_tensor("W_k_up", [DC, D], F32, kind="ExternalInput").ap()
    dr["W_v_up"] = nc.dram_tensor("W_v_up", [DC, D], F32, kind="ExternalInput").ap()
    dr["W_q"] = nc.dram_tensor("W_q", [D, D], F32, kind="ExternalInput").ap()
    dr["W_q_rope"] = nc.dram_tensor("W_q_rope", [D, DR], F32, kind="ExternalInput").ap()
    dr["W_k_rope"] = nc.dram_tensor("W_k_rope", [DC, DR], F32, kind="ExternalInput").ap()
    dr["W_o"] = nc.dram_tensor("W_o", [D, D], F32, kind="ExternalInput").ap()
    dr["wq"] = nc.dram_tensor("wq", [D, D], F32, kind="ExternalInput").ap()
    for k, v in _consts().items():
        dt = BF16 if v.dtype == ml_dtypes.bfloat16 else F32
        dr[k] = nc.dram_tensor(k, list(v.shape), dt, kind="ExternalInput").ap()
    dr["out"] = nc.dram_tensor("out", [T, D], F32, kind="ExternalOutput").ap()
    dr["scr"] = nc.dram_tensor("scr", [8 * D], F32, kind="Internal").ap()

    dbg = {}

    def tap(name, shape, dt=F32):
        if name in debug:
            dbg[name] = nc.dram_tensor("dbg_" + name, list(shape), dt,
                                       kind="ExternalOutput").ap()
            return dbg[name]
        return None

    for s in ("ldx", "ldg", "ldm0", "ldm1", "ldm2", "ldxa", "ldxb",
              "ldqa", "ldqb", "ldrb", "sT", "sV", "sS", "sG", "sGb", "st",
              "sta", "stb", "sD"):
        em.sem_alloc(s)

    def vdrain():
        n = em.now("sD")
        return ("sD", n)

    sc.push()
    xin = [sc.alloc(f"xin{i}", [128, D]) for i in range(2)]
    attn = [sc.alloc(f"attn{i}", [128, D], BF16) for i in range(NT)]
    ident = sc.alloc("identS", [128, 128], BF16)
    modcols = sc.alloc("modcols", [128, 48], F32)
    ca1 = sc.alloc("ca1", [128, 8], F32)
    onesrow = sc.alloc("onesrowS", [1, 128], F32)
    onescol = sc.alloc("onescol", [128, 1], F32)
    epscol = sc.alloc("epscol", [128, 1], F32)
    pcol = sc.alloc("pcol", [128, 1], F32)
    stat = sc.alloc("stat", [128, 12], F32)
    mv = sc.alloc("mv", [128, 2], F32)
    rstd = sc.alloc("rstd", [128, 1], F32)
    m2 = sc.alloc("m2", [128, 1], F32)
    rec = sc.alloc("rec", [128, 16], F32)
    knmax = sc.alloc("knmax", [8, 1], F32)
    ascall = sc.alloc("ascall", [128, NT], F32)
    scol = sc.alloc("scol", [128, 1], F32)
    w1 = sc.alloc("w1", [1, 1], F32)
    wscol = sc.alloc("wscol", [128, 1], F32)
    wrecip = sc.alloc("wrecip", [128, 1], F32)
    wsfin = sc.alloc("wsfin", [128, 1], F32)
    wsum = sc.alloc("wsum", [128, 9], F32)
    dbgcols = sc.alloc("dbgcols", [128, 48], F32)
    sqbuf = sc.alloc("sqbuf", [128, D], F32)
    aux1 = sc.alloc("aux1", [128, D], BF16)

    sc.push()
    h1T = sc.alloc("h1T", [128, 8 * T], BF16)
    WqH = sc.alloc("WqH", [128, 8 * 512], BF16)
    Wkv = sc.alloc("Wkv", [128, 8 * DC], BF16)
    Wkup = sc.alloc("Wkup", [128, D], BF16)
    Wvup = sc.alloc("Wvup", [128, D], BF16)
    Wqr = sc.alloc("Wqr", [128, 8 * DR], BF16)
    Wkr = sc.alloc("Wkr", [128, DR], BF16)
    negI = sc.alloc("negIS", [128, 128], BF16)
    U0 = sc.alloc("U0S", [128, 128], BF16)
    selcols = sc.alloc("selcolsS", [128, 64], BF16)
    ones32 = sc.alloc("ones32S", [32, 8], BF16)
    silu = sc.alloc("siluS", [128, 8], BF16)
    ccols = sc.alloc("ccols", [128, 8], F32)
    ckvT = sc.alloc("ckvT", [128, T], BF16)
    qropeR = sc.alloc("qropeR", [32, T], BF16)
    kropeR = sc.alloc("kropeR", [32, T], BF16)
    qrsq = sc.alloc("qrsq", [32, T], BF16)
    krsq = sc.alloc("krsq", [32, T], BF16)
    n1 = sc.alloc("n1", [128, D], BF16)

    sc.push()
    CS1 = sc.alloc("CS1s", [32, T], F32)
    CS2 = sc.alloc("CS2s", [32, T], F32)
    ropeT1 = sc.alloc("ropeT1", [32, T], F32)
    qraw = sc.alloc("qraw", [32, T], BF16)
    ropeswap = sc.alloc("ropeswap", [32, T], BF16)

    sc.push()
    modrow = sc.alloc("modrow", [1, 6 * D], F32)
    gtmp = sc.alloc("gtmp", [1, 2 * D], F32)
    wmodring = [sc.alloc(f"wmodring{i}", [128, 1024], BF16) for i in range(3)]

    SA = nc.alloc_psum_tensor("SA", [128, 1024], F32).ap()
    SB = nc.alloc_psum_tensor("SB", [128, 1024], F32).ap()
    OC1 = nc.alloc_psum_tensor("OC1", [128, 512], F32).ap()
    OC2 = nc.alloc_psum_tensor("OC2", [128, 512], F32).ap()
    OC3 = nc.alloc_psum_tensor("OC3", [128, 512], F32).ap()
    PT = nc.alloc_psum_tensor("PT", [128, 1024], BF16).ap()

    try:
        def g_dma(dst, src, sem="ldg", slow=False, wait=()):
            em.emit("gpsimd", lambda g, d=dst, s=src, sl=slow: g.dma_start(
                out=d, in_=s, allow_slow_non_contiguous=sl), wait=wait, inc=[(sem, 16)])

        def s_dma(dst, src, sem="ldx", slow=False, wait=()):
            em.emit("sync", lambda g, d=dst, s=src, sl=slow: g.dma_start(
                out=d, in_=s, allow_slow_non_contiguous=sl), wait=wait, inc=[(sem, 16)])

        g_dma(ident, dr["ident"]); g_dma(negI, dr["negI"]); g_dma(U0, dr["U0"])
        g_dma(selcols, dr["selcols"]); g_dma(ones32, dr["ones32"])
        g_dma(WqH.rearrange("p (ic o) -> p ic o", ic=8),
              dr["W_q"][:, 0:512].rearrange("(ic p) o -> p ic o", p=128))
        g_dma(Wkv.rearrange("p (ic o) -> p ic o", ic=8),
              dr["W_kv_down"].rearrange("(ic p) o -> p ic o", p=128))
        g_dma(Wkup, dr["W_k_up"])
        g_dma(Wvup, dr["W_v_up"])
        g_dma(Wqr.rearrange("p (ic o) -> p ic o", ic=8),
              dr["W_q_rope"].rearrange("(ic p) o -> p ic o", p=128))
        g_dma(Wkr, dr["W_k_rope"])
        n_ldg_pre = em.now("ldg")

        s_dma(CS1, dr["CS1c"]); s_dma(CS2, dr["CS2c"]); s_dma(onesrow, dr["onesrow"])
        s_dma(ccols, dr["c"].rearrange("(ic p) -> p ic", p=128), slow=True)
        n_ldx_c = em.now("ldx")
        em.emit("vector", lambda v: v.memset(onescol, 1.0))
        em.emit("vector", lambda v: v.memset(epscol, EPS_LN))
        em.emit("vector", lambda v: v.memset(pcol, 1.0), inc=[("sD", 1)])
        em.emit("vector", lambda v: v.tensor_scalar(
            out=pcol, in0=pcol, scalar1=C64, scalar2=-C64, op0=ALU.add, op1=ALU.add),
            wait=[vdrain()], inc=[("sV", 1)])
        nv_onescol = em.now("sV")
        em.emit("scalar", lambda s: s.activation(out=silu, in_=ccols, func=AF.Silu),
                wait=[("ldx", n_ldx_c)], inc=[("sS", 1)])
        ns_silu = em.now("sS")

        modevac = []
        modmm_done = []
        for ci in range(48):
            fs, kc = divmod(ci, 8)
            ring = wmodring[ci % 3]
            src = dr["wmod"][kc * 128:(kc + 1) * 128, fs * 1024:(fs + 1) * 1024]
            wt = []
            if ci >= 3:
                wt.append(("sT", modmm_done[ci - 3]))
            lsem = f"ldm{ci % 3}"
            em.emit("gpsimd", lambda g, d=ring, s=src: g.dma_start(out=d, in_=s),
                    wait=wt, inc=[(lsem, 16)])
            n_ld = em.now(lsem)
            for half in range(2):
                bank = OC1 if ((2 * fs + half) % 2 == 0) else OC2
                sidx = 2 * fs + half
                wt2 = [(lsem, n_ld), ("sS", ns_silu)]
                if kc == 0 and sidx >= 2:
                    wt2.append(("sS", modevac[sidx - 2]))
                em.emit("tensor", lambda t, b=bank, r=ring, h=half, kc=kc: t.matmul(
                    b[0:1, :], silu[:, kc:kc + 1], r[:, h * 512:(h + 1) * 512],
                    start=(kc == 0), stop=(kc == 7), skip_group_check=True),
                    wait=wt2, inc=[("sT", 1)])
                if kc == 7:
                    nmm = em.now("sT")
                    em.emit("scalar", lambda s, b=bank, o=sidx: s.copy(
                        out=modrow[0:1, o * 512:(o + 1) * 512], in_=b[0:1, :]),
                        wait=[("sT", nmm)], inc=[("sS", 1)])
                    modevac.append(em.now("sS"))
            modmm_done.append(em.now("sT"))
        ns_modrow = em.now("sS")

        for off in (2 * D, 5 * D):
            srcr = modrow[0:1, off:off + D]
            sgn = gtmp[0:1, 0:D]
            mag = gtmp[0:1, D:2 * D]
            em.emit("vector", lambda v, s=srcr, d=sgn: v.tensor_scalar(
                out=d, in0=s, scalar1=0.0, scalar2=None, op0=ALU.is_ge),
                wait=[("sS", ns_modrow)])
            em.emit("vector", lambda v, d=sgn: v.tensor_scalar(
                out=d, in0=d, scalar1=2.0, scalar2=-1.0, op0=ALU.mult, op1=ALU.add))
            em.emit("vector", lambda v, s=srcr, a=sgn, d=mag: v.tensor_tensor(
                out=d, in0=s, in1=a, op=ALU.mult))
            em.emit("vector", lambda v, d=mag: v.tensor_scalar(
                out=d, in0=d, scalar1=0.25, scalar2=None, op0=ALU.max))
            em.emit("vector", lambda v, s=srcr, a=sgn, b=mag: v.tensor_tensor(
                out=s, in0=a, in1=b, op=ALU.mult), inc=[("sV", 1)])
        em.emit("vector", lambda v: v.tensor_scalar(
            out=modrow[0:1, 4 * D:5 * D], in0=modrow[0:1, 4 * D:5 * D],
            scalar1=1.0, scalar2=None, op0=ALU.add), inc=[("sV", 1)])
        nv_rows = em.now("sV")

        em.emit("sync", lambda g: g.dma_start(out=dr["scr"][0:6 * D], in_=modrow),
                wait=[("sV", nv_rows)], inc=[("sG", 16)])
        em.emit("sync", lambda g: g.dma_start(
            out=modcols, in_=dr["scr"][0:6 * D].rearrange("(c p) -> p c", p=128),
            allow_slow_non_contiguous=True),
            wait=[("sG", em.now("sG"))], inc=[("sG", 16)])
        ng_cols = em.now("sG")
        em.emit("vector", lambda v: v.tensor_scalar(
            out=ca1, in0=modcols[:, 8:16], scalar1=1.0, scalar2=None, op0=ALU.add),
            wait=[("sG", ng_cols)], inc=[("sV", 1)])
        nv_ca1 = em.now("sV")
        if tap("modcols", [128, 48]) is not None:
            em.emit("sync", lambda g: g.dma_start(out=dbg["modcols"], in_=modcols),
                    wait=[("sG", ng_cols)], inc=[("st", 16)])
        if tap("modrow", [1, 6 * D]) is not None:
            em.emit("sync", lambda g: g.dma_start(out=dbg["modrow"], in_=modrow),
                    wait=[("sV", nv_rows)], inc=[("st", 16)])

        n1_done = []
        pt_evac = []
        ns_rstd_prev = 0
        ntr_prev = 0
        nv_mv_prev = 0
        ns_sq_prev = 0
        cutpoint("p1a")
        ns_sq_hist = []
        for i in range(NT):
            wt = []
            if i >= 2:
                wt.append(("sV", n1_done[i - 2]))
            xsem = "ldxa" if i % 2 == 0 else "ldxb"
            em.emit("sync", lambda g, i=i: g.dma_start(
                out=xin[i % 2], in_=dr["x"][i * 128:(i + 1) * 128, :]),
                wait=wt, inc=[(xsem, 16)])
            nld = em.now(xsem)
            if i == 0 and "xf32" in debug:
                dbg["xf32"] = nc.dram_tensor("dbg_xf32", [128, D], F32,
                                             kind="ExternalOutput").ap()
                em.emit("sync", lambda g: g.dma_start(out=dbg["xf32"], in_=xin[0]),
                        wait=[(xsem, nld)], inc=[("st", 16)])
            xi = xin[i % 2]
            em.emit("vector", lambda v, x=xi: v.tensor_tensor(
                out=sqbuf, in0=x, in1=x, op=ALU.mult),
                wait=[(xsem, nld)], inc=[("sV", 1)])
            em.emit("vector", lambda v: v.tensor_reduce(
                out=stat[:, 1:2], in_=sqbuf, axis=AX.X, op=ALU.add))
            em.emit("vector", lambda v, x=xi: v.tensor_reduce(
                out=stat[:, 0:1], in_=x, axis=AX.X, op=ALU.add),
                inc=[("sD", 1)])
            em.emit("vector", lambda v: v.tensor_scalar(
                out=mv[:, 0:1], in0=stat[:, 0:1], scalar1=1.0 / D, scalar2=None,
                op0=ALU.mult), wait=[vdrain()], inc=[("sD", 1)])
            if i < 2:
                em.emit("vector", lambda v, i=i: v.tensor_copy(
                    out=dbgcols[:, 44 + 2 * i:45 + 2 * i], in_=stat[:, 0:1]))
                em.emit("vector", lambda v, i=i: v.tensor_copy(
                    out=dbgcols[:, 45 + 2 * i:46 + 2 * i], in_=stat[:, 1:2]))
            em.emit("vector", lambda v: v.tensor_tensor(
                out=stat[:, 2:3], in0=mv[:, 0:1], in1=mv[:, 0:1], op=ALU.mult),
                wait=[vdrain()], inc=[("sD", 1)])
            em.emit("vector", lambda v: v.scalar_tensor_tensor(
                out=mv[:, 1:2], in0=stat[:, 1:2], scalar=1.0 / D, in1=stat[:, 2:3],
                op0=ALU.mult, op1=ALU.subtract), wait=[vdrain()], inc=[("sV", 1)])
            nv_mv = em.now("sV")
            nv_mv_prev = nv_mv
            em.emit("scalar", lambda s: s.activation(out=rstd, in_=mv[:, 1:2],
                    func=AF.Ln, bias=epscol), wait=[("sV", nv_mv), ("sV", nv_onescol)])
            em.emit("scalar", lambda s: s.activation(out=rstd, in_=rstd,
                    func=AF.Exp, scale=-0.5), inc=[("sS", 1)])
            ns_rstd = em.now("sS")
            ns_rstd_prev = ns_rstd
            em.emit("vector", lambda v: v.scalar_tensor_tensor(
                out=m2, in0=mv[:, 0:1], scalar=-1.0, in1=rstd,
                op0=ALU.mult, op1=ALU.mult), wait=[("sS", ns_rstd)], inc=[("sD", 1)])
            em.emit("vector", lambda v, i=i: v.tensor_copy(
                out=dbgcols[:, i:i + 1], in_=rstd))
            em.emit("vector", lambda v, i=i: v.tensor_copy(
                out=dbgcols[:, 16 + i:17 + i], in_=mv[:, 1:2]))
            em.emit("vector", lambda v, i=i: v.tensor_copy(
                out=dbgcols[:, 32 + i:33 + i], in_=m2))
            em.emit("vector", lambda v, x=xi: v.tensor_scalar(
                out=n1, in0=x, scalar1=rstd, scalar2=m2, op0=ALU.mult, op1=ALU.add),
                wait=[vdrain()] + ([("sT", ntr_prev)] if i else []), inc=[("sV", 1)])
            n1_done.append(em.now("sV"))
            wt2 = [("sV", n1_done[-1]), ("ldg", n_ldg_pre)]
            if i > 0:
                wt2.append(("sS", pt_evac[i - 1]))
            for fc in range(8):
                em.emit("tensor", lambda t, fc=fc: t.transpose(
                    PT[:, fc * 128:(fc + 1) * 128], n1[:, fc * 128:(fc + 1) * 128], ident),
                    wait=(wt2 if fc == 0 else ()), inc=([("sT", 1)] if fc == 7 else []))
            ntr = em.now("sT")
            ntr_prev = ntr
            for fc in range(8):
                em.emit("scalar", lambda s, fc=fc, i=i: s.copy(
                    out=h1T[:, fc * T + i * 128: fc * T + (i + 1) * 128],
                    in_=PT[:, fc * 128:(fc + 1) * 128]),
                    wait=([("sT", ntr)] if fc == 0 else ()),
                    inc=([("sS", 1)] if fc == 7 else []))
            pt_evac.append(em.now("sS"))
        ns_h1raw = em.now("sS")
        pt_rel = [("sS", ns_h1raw)]

        for fc in range(8):
            if "h1mod" in SKIP:
                em.emit("vector", lambda v: v.engine_nop(),
                        wait=([("sS", ns_h1raw), ("sV", nv_ca1)] if fc == 0 else ()),
                        inc=([("sV", 1)] if fc == 7 else []))
                continue
            em.emit("vector", lambda v, fc=fc: v.tensor_scalar(
                out=h1T[:, fc * T:(fc + 1) * T], in0=h1T[:, fc * T:(fc + 1) * T],
                scalar1=ca1[:, fc:fc + 1], scalar2=modcols[:, fc:fc + 1],
                op0=ALU.mult, op1=ALU.add),
                wait=([("sS", ns_h1raw), ("sV", nv_ca1)] if fc == 0 else ()),
                inc=([("sV", 1)] if fc == 7 else []))
        nv_h1 = em.now("sV")
        if tap("dbgcols", [128, 48]) is not None:
            em.emit("sync", lambda g: g.dma_start(out=dbg["dbgcols"], in_=dbgcols),
                    wait=[("sV", nv_h1)], inc=[("st", 16)])
        cutpoint("p1b")
        if tap("h1T", [128, 8 * T], BF16) is not None:
            em.emit("sync", lambda g: g.dma_start(out=dbg["h1T"], in_=h1T),
                    wait=[("sV", nv_h1)], inc=[("st", 16)])

        pslices = [SA[:, 0:512], SA[:, 512:1024], SB[:, 0:512], SB[:, 512:1024]]
        prev_rel = [None, None, None, None]
        psl_i = [0]

        def next_slice(extra_wait=()):
            sl = psl_i[0] % 4
            psl_i[0] += 1
            wt = list(extra_wait)
            if prev_rel[sl] is not None:
                wt.append(prev_rel[sl])
            return sl, pslices[sl], wt

        for qc in range(4):
            sl, pp, wt = next_slice([("sV", nv_h1)])
            for ic in range(8):
                em.emit("tensor", lambda t, ic=ic, pp=pp, qc=qc: t.matmul(
                    pp, Wkv[:, ic * DC:(ic + 1) * DC],
                    h1T[:, ic * T + qc * 512: ic * T + (qc + 1) * 512],
                    start=(ic == 0), stop=(ic == 7), skip_group_check=True),
                    wait=(wt if ic == 0 else ()), inc=([("sT", 1)] if ic == 7 else []))
            nmm = em.now("sT")
            em.emit("vector", lambda v, pp=pp, qc=qc: v.tensor_copy(
                out=ckvT[:, qc * 512:(qc + 1) * 512], in_=pp),
                wait=[("sT", nmm)], inc=[("sV", 1)])
            prev_rel[sl] = ("sV", em.now("sV"))
        nv_ckv = em.now("sV")

        for qc in range(4):
            sl, pp, wt = next_slice([("sV", nv_h1)])
            for ic in range(8):
                em.emit("tensor", lambda t, ic=ic, pp=pp, qc=qc: t.matmul(
                    pp[0:32, :], Wqr[:, ic * DR:(ic + 1) * DR],
                    h1T[:, ic * T + qc * 512: ic * T + (qc + 1) * 512],
                    start=(ic == 0), stop=(ic == 7), skip_group_check=True),
                    wait=(wt if ic == 0 else ()), inc=([("sT", 1)] if ic == 7 else []))
            nmm = em.now("sT")
            em.emit("vector", lambda v, pp=pp, qc=qc: v.tensor_copy(
                out=qraw[:, qc * 512:(qc + 1) * 512], in_=pp[0:32, :]),
                wait=[("sT", nmm)], inc=[("sV", 1)])
            prev_rel[sl] = ("sV", em.now("sV"))
        nv_qraw = em.now("sV")

        for qc in range(4):
            sl, pp, wt = next_slice([("sV", nv_ckv)])
            em.emit("tensor", lambda t, pp=pp, qc=qc: t.matmul(
                pp[0:32, :], Wkr, ckvT[:, qc * 512:(qc + 1) * 512],
                start=True, stop=True), wait=wt, inc=[("sT", 1)])
            nmm = em.now("sT")
            em.emit("vector", lambda v, pp=pp, qc=qc: v.tensor_copy(
                out=krsq[:, qc * 512:(qc + 1) * 512], in_=pp[0:32, :]),
                wait=[("sT", nmm)], inc=[("sV", 1)])
            prev_rel[sl] = ("sV", em.now("sV"))
        nv_kraw = em.now("sV")

        def rope_rotate(raw, out, wait_n):
            em.emit("vector", lambda v, r=raw: v.stream_shuffle(
                out=ropeswap, in_=r, mask=[(i + 16) % 32 for i in range(32)]),
                wait=[("sV", wait_n), ("ldx", n_ldx_c)])
            em.emit("vector", lambda v, r=raw, o=out: v.tensor_tensor(
                out=o, in0=r, in1=CS1, op=ALU.mult))
            em.emit("vector", lambda v: v.tensor_tensor(
                out=ropeT1, in0=ropeswap, in1=CS2, op=ALU.mult))
            em.emit("vector", lambda v, o=out: v.tensor_tensor(
                out=o, in0=o, in1=ropeT1, op=ALU.add), inc=[("sV", 1)])
            return em.now("sV")
        nv_qrot = rope_rotate(qraw, qropeR, nv_qraw)
        nv_krot = rope_rotate(krsq, kropeR, nv_kraw)
        em.emit("scalar", lambda s: s.activation(out=qrsq, in_=qropeR, func=AF.Square),
                wait=[("sV", nv_qrot)], inc=[("sS", 1)])
        em.emit("scalar", lambda s: s.activation(out=krsq, in_=kropeR, func=AF.Square),
                wait=[("sV", nv_krot)], inc=[("sS", 1)])
        ns_ropesq = em.now("sS")
        if tap("qropeR", [32, T], BF16) is not None:
            em.emit("sync", lambda g: g.dma_start(out=dbg["qropeR"], in_=qropeR),
                    wait=[("sV", nv_qrot)], inc=[("st", 16)])
        if tap("ckvT", [128, T], BF16) is not None:
            em.emit("sync", lambda g: g.dma_start(out=dbg["ckvT"], in_=ckvT),
                    wait=[("sV", nv_ckv)], inc=[("st", 16)])

        cutpoint("p2")
        sc.pop()
        sc.pop()

        sc.push()
        qaug = sc.alloc("qaug", [97, 8 * T], BF16)
        kaug = sc.alloc("kaug", [97, 8 * T], BF16)
        vaug = sc.alloc("vaug", [128, NT * 8 * VS], BF16)
        E = [sc.alloc(f"E{i}", [128, 1024], BF16) for i in range(2)]
        pn = sc.alloc("pn", [8, T], BF16)
        qsqt = [sc.alloc(f"qsqt{i}", [128, 512], BF16) for i in range(2)]

        oc3_rel = [None]
        last_exp_par = [None, None]
        last_pv_E = [None, None]
        par_i = [0]
        for H in range(2):
            exp_guard = ("sS", em.now("sS")) if H == 1 else None
            if H == 1:
                g_dma(WqH.rearrange("p (ic o) -> p ic o", ic=8),
                      dr["W_q"][:, 512:1024].rearrange("(ic p) o -> p ic o", p=128),
                      wait=[("sS", em.now("sS"))])
            n_ldg_wq = em.now("ldg")

            if "ms" not in SKIP:
                em.emit("vector", lambda v: v.memset(
                    vaug.rearrange("p (s c) -> p s c", c=VS)[:, :, 64:65], 1.0),
                    wait=[("sG", ng_cols)] + ([exp_guard] if exp_guard else []))
                em.emit("vector", lambda v: v.memset(
                    vaug.rearrange("p (s c) -> p s c", c=VS)[:, :, 65:66], 0.0),
                    inc=[("sV", 1)])
                em.emit("vector", lambda v: v.memset(kaug[96:97, :], 1.0), inc=[("sV", 1)])

            pn_guard = [None, None]
            for qc in range(int(os.environ.get("KQC", "4"))):
                kn_mm_last = [None]
                wt = [("sS", ns_ropesq)]
                if oc3_rel[0] is not None:
                    wt.append(oc3_rel[0])
                if "pn" not in SKIP and "pnrope" not in SKIP:
                    em.emit("tensor", lambda t, qc=qc: t.matmul(
                        OC3[0:8, :], ones32, krsq[:, qc * 512:(qc + 1) * 512],
                        start=True, stop=False, skip_group_check=True), wait=wt)
                for oc in range(4):
                    sl, pp, wt2 = next_slice([("sV", nv_ckv), exp_guard])
                    ocol = H * 512 + oc * 128
                    em.emit("tensor", lambda t, pp=pp, qc=qc, ocol=ocol: t.matmul(
                        pp, Wkup[:, ocol:ocol + 128], ckvT[:, qc * 512:(qc + 1) * 512],
                        start=True, stop=True), wait=wt2, inc=[("sT", 1)])
                    nmm = em.now("sT")
                    for sub in range(2):
                        blk = 2 * oc + sub
                        em.emit("vector", lambda v, pp=pp, sub=sub, blk=blk, qc=qc: v.tensor_copy(
                            out=kaug[0:64, blk * T + qc * 512: blk * T + (qc + 1) * 512],
                            in_=pp[sub * 64:(sub + 1) * 64, :]),
                            wait=([("sT", nmm)] if sub == 0 else ()), inc=[("sV", 1)])
                    nev = em.now("sV")
                    if True:
                        if True:
                            em.emit("scalar", lambda s, pp=pp, oc=oc: s.activation(
                                out=qsqt[oc % 2], in_=pp, func=AF.Square),
                                wait=[("sT", nmm), ("sV", nev), pn_guard[oc % 2]], inc=[("sS", 1)])
                        nsq = em.now("sS")
                        if "pnsel" not in SKIP:
                            em.emit("tensor", lambda t, oc=oc: t.matmul(
                                OC3[0:8, :], selcols[:, oc * 8:(oc + 1) * 8], qsqt[oc % 2],
                                start=("pnrope" in SKIP and oc == 0),
                                stop=(oc == 3), skip_group_check=True),
                                wait=[("sS", nsq)], inc=[("sT", 1)])
                            pn_guard[oc % 2] = ("sT", em.now("sT"))
                    prev_rel[sl] = ("sV", em.now("sV"))
                    if oc == 3:
                        kn_mm_last[0] = em.now("sT")
                if "pn" not in SKIP:
                    em.emit("vector", lambda v, qc=qc: v.tensor_copy(
                        out=pn[0:8, qc * 512:(qc + 1) * 512], in_=OC3[0:8, :]),
                        wait=[("sT", kn_mm_last[0])], inc=[("sV", 1)])
                    oc3_rel[0] = ("sV", em.now("sV"))
            nv_kn = em.now("sV")
            cutpoint(f"p3k_{H}")
            em.emit("vector", lambda v: v.tensor_reduce(
                out=knmax, in_=pn[0:8, :], axis=AX.X, op=ALU.max),
                wait=[("sV", nv_kn)], inc=[("sV", 1)])
            em.emit("scalar", lambda s: s.sqrt(out=knmax[0:8, :], in_=knmax[0:8, :]),
                    wait=[("sV", em.now("sV"))], inc=[("sS", 1)])
            em.emit("vector", lambda v: v.tensor_scalar(
                out=knmax[0:8, :], in0=knmax[0:8, :], scalar1=-1.0, scalar2=None,
                op0=ALU.mult), wait=[("sS", em.now("sS"))], inc=[("sV", 1)])
            nv_knm = em.now("sV")
            cutpoint(f"p3km_{H}")

            for qc in range(4):
                qn_mm_last = [None]
                wt = [("sS", ns_ropesq), ("sV", nv_knm)]
                if oc3_rel[0] is not None:
                    wt.append(oc3_rel[0])
                em.emit("tensor", lambda t, qc=qc: t.matmul(
                    OC3[0:8, :], ones32, qrsq[:, qc * 512:(qc + 1) * 512],
                    start=True, stop=False, skip_group_check=True), wait=wt)
                for oc in range(4):
                    sl, pp, wt2 = next_slice([("sV", nv_h1), ("ldg", n_ldg_wq), exp_guard])
                    ocol = oc * 128
                    for ic in range(8):
                        em.emit("tensor", lambda t, ic=ic, pp=pp, qc=qc, ocol=ocol: t.matmul(
                            pp, WqH[:, ic * 512 + ocol: ic * 512 + ocol + 128],
                            h1T[:, ic * T + qc * 512: ic * T + (qc + 1) * 512],
                            start=(ic == 0), stop=(ic == 7), skip_group_check=True),
                            wait=(wt2 if ic == 0 else ()), inc=([("sT", 1)] if ic == 7 else []))
                    nmm = em.now("sT")
                    for sub in range(2):
                        blk = 2 * oc + sub
                        em.emit("vector", lambda v, pp=pp, sub=sub, blk=blk, qc=qc: v.tensor_copy(
                            out=qaug[0:64, blk * T + qc * 512: blk * T + (qc + 1) * 512],
                            in_=pp[sub * 64:(sub + 1) * 64, :]),
                            wait=([("sT", nmm)] if sub == 0 else ()), inc=[("sV", 1)])
                    nev = em.now("sV")
                    em.emit("scalar", lambda s, pp=pp, oc=oc: s.activation(
                        out=qsqt[oc % 2], in_=pp, func=AF.Square),
                        wait=[("sT", nmm), ("sV", nev), pn_guard[oc % 2]], inc=[("sS", 1)])
                    nsq = em.now("sS")
                    em.emit("tensor", lambda t, oc=oc: t.matmul(
                        OC3[0:8, :], selcols[:, oc * 8:(oc + 1) * 8], qsqt[oc % 2],
                        start=False, stop=(oc == 3), skip_group_check=True),
                        wait=[("sS", nsq)], inc=[("sT", 1)])
                    pn_guard[oc % 2] = ("sT", em.now("sT"))
                    prev_rel[sl] = ("sV", em.now("sV"))
                    if oc == 3:
                        qn_mm_last[0] = em.now("sT")
                em.emit("vector", lambda v, qc=qc: v.tensor_copy(
                    out=pn[0:8, qc * 512:(qc + 1) * 512], in_=OC3[0:8, :]),
                    wait=[("sT", qn_mm_last[0])], inc=[("sV", 1)])
                oc3_rel[0] = ("sV", em.now("sV"))
                em.emit("scalar", lambda s, qc=qc: s.sqrt(
                    out=pn[0:8, qc * 512:(qc + 1) * 512],
                    in_=pn[0:8, qc * 512:(qc + 1) * 512]),
                    wait=[("sV", em.now("sV"))], inc=[("sS", 1)])
            ns_qn = em.now("sS")
            cutpoint(f"p3q_{H}")
            em.emit("vector", lambda v: v.tensor_scalar(
                out=pn[0:8, :], in0=pn[0:8, :], scalar1=knmax[0:8, 0:1], scalar2=None,
                op0=ALU.mult), wait=[("sS", ns_qn)], inc=[("sV", 1)])
            nv_B = em.now("sV")

            for kt in range(NT):
                sl, pp, wt = next_slice([("sV", nv_ckv), exp_guard])
                em.emit("tensor", lambda t, pp=pp, kt=kt, H=H: t.matmul(
                    pp, ckvT[:, kt * 128:(kt + 1) * 128], Wvup[:, H * 512:H * 512 + 512],
                    start=True, stop=True), wait=wt, inc=[("sT", 1)])
                nmm = em.now("sT")
                em.emit("vector", lambda v, pp=pp, kt=kt: v.tensor_copy(
                    out=vaug.rearrange("p (s c) -> p s c", c=VS)[:, kt * 8:(kt + 1) * 8, 0:64],
                    in_=pp), wait=[("sT", nmm)], inc=[("sV", 1)])
                prev_rel[sl] = ("sV", em.now("sV"))
            nv_v = em.now("sV")
            cutpoint(f"p3v_{H}")

            for par in range(2):
                em.emit("gpsimd", lambda g, par=par: g.dma_start(
                    out=qaug[96:97, :].rearrange("o (b t) -> o b t", t=T)[:, par::2, :],
                    in_=pn[0:4, :]), wait=[("sV", nv_B)], inc=[("sGb", 16)])
            ng_scatter = em.now("sGb")
            cutpoint(f"p3s_{H}")
            for blk in range(8):
                em.emit("vector", lambda v, blk=blk: v.tensor_copy(
                    out=qaug[64:96, blk * T:(blk + 1) * T], in_=qropeR),
                    wait=([("sV", nv_qrot)] if blk == 0 else ()), inc=[("sV", 1)])
                em.emit("vector", lambda v, blk=blk: v.tensor_copy(
                    out=kaug[64:96, blk * T:(blk + 1) * T], in_=kropeR),
                    wait=([("sV", nv_krot)] if blk == 0 else ()), inc=[("sV", 1)])
            nv_aug = em.now("sV")

            if H == 0 and tap("qaug", [97, 8 * T], BF16) is not None:
                em.emit("sync", lambda g: g.dma_start(out=dbg["qaug"], in_=qaug),
                        wait=[("sV", nv_aug), ("sGb", ng_scatter)], inc=[("st", 16)])
                tap("kaug", [97, 8 * T], BF16)
                em.emit("sync", lambda g: g.dma_start(out=dbg["kaug"], in_=kaug),
                        wait=[("sV", nv_aug)], inc=[("st", 16)])
            if H == 0 and tap("vaug", [128, NT * 8 * VS], BF16) is not None:
                em.emit("sync", lambda g: g.dma_start(out=dbg["vaug"], in_=vaug),
                        wait=[("sV", nv_v)], inc=[("st", 16)])

            cutpoint(f"p3a_{H}")
            oc1_rel = [None]
            oc2_rel = [None]
            for hh in range(8):
                blk = hh
                bank_opened = {"OC1": False, "OC2": False, "OC3": False}
                diag_inc = {}
                for kt in range(NT):
                    for qh in range(2):
                        lo = max(kt * 128, qh * 1024)
                        hi = (qh + 1) * 1024
                        if lo >= hi:
                            continue
                        p = par_i[0] % 2
                        par_i[0] += 1
                        SP = (SA, SB)[p]
                        al = (lo // 512) * 512
                        nsl = (hi - al) // 512
                        first = True
                        for si in range(nsl):
                            soff = al - qh * 1024 + si * 512
                            qcol = blk * T + al + si * 512
                            has_mask = (si == 0) and (kt * 128 >= lo)
                            wtS = []
                            if first:
                                if last_exp_par[p] is not None:
                                    wtS.append(("sS", last_exp_par[p]))
                                wtS += [("sV", nv_aug), ("sGb", ng_scatter),
                                        ("sV", nv_v)]
                                first = False
                            em.emit("tensor", lambda t, SP=SP, soff=soff, qcol=qcol, blk=blk, kt=kt, hm=has_mask: t.matmul(
                                SP[:, soff:soff + 512],
                                kaug[0:97, blk * T + kt * 128: blk * T + (kt + 1) * 128],
                                qaug[0:97, qcol:qcol + 512],
                                start=True, stop=not hm, skip_group_check=True),
                                wait=wtS,
                                inc=([("sT", 1)] if (si == nsl - 1 and not has_mask) else []))
                            if has_mask:
                                mo = kt * 128 - qh * 1024
                                em.emit("tensor", lambda t, SP=SP, mo=mo: t.matmul(
                                    SP[:, mo:mo + 128], negI, U0,
                                    start=False, stop=True, skip_group_check=True),
                                    inc=([("sT", 1)] if si == nsl - 1 else []))
                        n_sgrp = em.now("sT")
                        wte = [("sT", n_sgrp)]
                        if last_pv_E[p] is not None:
                            wte.append(("sT", last_pv_E[p]))
                        em.emit("scalar", lambda s, SP=SP, p=p, lo=lo, qh=qh: s.activation(
                            out=E[p][:, lo - qh * 1024: 1024],
                            in_=SP[:, lo - qh * 1024: 1024],
                            func=AF.Exp, scale=SCALE), wait=wte, inc=[("sS", 1)])
                        last_exp_par[p] = em.now("sS")
                        n_exp = em.now("sS")
                        qs_lo = max(kt, qh * 8)
                        qs_hi = min(15, qh * 8 + 7)
                        firstpv = True
                        for qs in range(qs_lo, qs_hi + 1):
                            if qs < 7:
                                bank, slot, bname, brel = OC1, qs, "OC1", oc1_rel
                            elif qs < 14:
                                bank, slot, bname, brel = OC2, qs - 7, "OC2", oc2_rel
                            else:
                                bank, slot, bname, brel = OC3, qs - 14, "OC3", oc3_rel
                            wtp = []
                            if firstpv:
                                wtp.append(("sS", n_exp))
                                firstpv = False
                            sess = False
                            if not bank_opened[bname]:
                                bank_opened[bname] = True
                                sess = True
                                if brel[0] is not None:
                                    wtp.append(brel[0])
                            is_diag = (kt == qs)
                            inc = [("sT", 1)] if (is_diag or qs == qs_hi) else []
                            em.emit("tensor", lambda t, bank=bank, slot=slot, p=p, qs=qs, qh=qh, kt=kt, hh=hh, ss=sess, dg=is_diag: t.matmul(
                                bank[:, slot * VS: slot * VS + VS],
                                E[p][:, qs * 128 - qh * 1024: qs * 128 - qh * 1024 + 128],
                                vaug[:, (kt * 8 + hh) * VS: (kt * 8 + hh) * VS + VS],
                                start=ss, stop=dg, skip_group_check=True),
                                wait=wtp, inc=inc)
                            if is_diag:
                                diag_inc[qs] = em.now("sT")
                        last_pv_E[p] = em.now("sT")

                    def emit_evacs(qs_list, brel):
                        def bk(qs):
                            if qs < 7:
                                return OC1, qs
                            if qs < 14:
                                return OC2, qs - 7
                            return OC3, qs - 14
                        for qs in qs_list:
                            bank, slot = bk(qs)
                            em.emit("vector", lambda v, bank=bank, slot=slot, qs=qs: v.reciprocal(
                                out=rec[:, qs:qs + 1],
                                in_=bank[:, slot * VS + 64: slot * VS + 65]),
                                wait=([("sT", diag_inc[qs_list[-1]])] if qs == qs_list[0] else ()),
                                inc=([("sD", 1)] if qs == qs_list[-1] and len(qs_list) == 1 else []))
                        wd = vdrain() if len(qs_list) == 1 else None
                        for qs in qs_list:
                            bank, slot = bk(qs)
                            em.emit("vector", lambda v, bank=bank, slot=slot, qs=qs, hh=hh, H=H: v.tensor_scalar(
                                out=attn[qs][:, (8 * H + hh) * 64:(8 * H + hh) * 64 + 64],
                                in0=bank[:, slot * VS: slot * VS + 64],
                                scalar1=rec[:, qs:qs + 1], scalar2=None, op0=ALU.mult),
                                wait=(wd,) if (wd and qs == qs_list[0]) else (),
                                inc=[("sV", 1)])
                        brel[0] = ("sV", em.now("sV"))
                    if kt == 7:
                        emit_evacs(list(range(0, 7)), oc1_rel)
                    if kt == 14:
                        emit_evacs(list(range(7, 14)), oc2_rel)
                    if kt == 15:
                        emit_evacs([14, 15], oc3_rel)

            cutpoint(f"p3b_{H}")
        nv_attn = em.now("sV")
        ns_attn = em.now("sS")
        sc.pop()

        if tap("attn0", [128, D], BF16) is not None:
            em.emit("sync", lambda g: g.dma_start(out=dbg["attn0"], in_=attn[0]),
                    wait=[("sV", nv_attn)], inc=[("st", 16)])
            tap("attn15", [128, D], BF16)
            em.emit("sync", lambda g: g.dma_start(out=dbg["attn15"], in_=attn[15]),
                    wait=[("sV", nv_attn)], inc=[("st", 16)])

        sc.pop()

        sc.push()
        x1 = [sc.alloc(f"x1_{i}", [128, D]) for i in range(NT)]
        Wo = sc.alloc("Wo", [128, 8 * D], BF16)
        wqT = sc.alloc("wqT", [128, 8 * D], BF16)
        wqring = [sc.alloc(f"wqring{i}", [128, D]) for i in range(2)]
        attnTr = [sc.alloc(f"attnTr{i}", [128, 8 * 128], BF16) for i in range(2)]
        aqTr = [sc.alloc(f"aqTr{i}", [128, 8 * 128], BF16) for i in range(2)]
        otile = [sc.alloc(f"otile{i}", [128, D]) for i in range(2)]
        h2a = sc.alloc("h2a", [128, D], F32)
        h2 = sc.alloc("h2", [128, D], F32)
        aqbf = sc.alloc("aqbf", [128, D], BF16)
        wqbf = sc.alloc("wqbf", [128, D], BF16)
        rnd_a = sc.alloc("rnd_a", [128, D], F32)
        rnd_b = sc.alloc("rnd_b", [128, D], F32)
        gab = sc.alloc("gab", [128, D], F32)
        gbb = sc.alloc("gbb", [128, D], F32)
        cbb = sc.alloc("cbb", [128, D], F32)
        sbb = sc.alloc("sbb", [128, D], F32)
        rowbuf = sc.alloc("rowbuf", [1, D], F32)

        bt_specs = [(gab, 2 * D), (gbb, 5 * D), (cbb, 4 * D), (sbb, 3 * D)]
        prev_bc = None
        for dst, off in bt_specs:
            em.emit("sync", lambda g, off=off: g.dma_start(
                out=rowbuf, in_=dr["scr"][off:off + D]),
                wait=[("sG", ng_cols), ("sV", nv_attn)] + ([prev_bc] if prev_bc else []),
                inc=[("ldrb", 16)])
            nld = em.now("ldrb")
            for half in range(2):
                wt = [("ldrb", nld), oc3_rel[0]]
                em.emit("tensor", lambda t, h=half: t.matmul(
                    OC3, onesrow, rowbuf[0:1, h * 512:(h + 1) * 512],
                    start=True, stop=True), wait=wt, inc=[("sT", 1)])
                nmm = em.now("sT")
                em.emit("vector", lambda v, d=dst, h=half: v.tensor_copy(
                    out=d[:, h * 512:(h + 1) * 512], in_=OC3),
                    wait=[("sT", nmm)], inc=[("sV", 1)])
                oc3_rel[0] = ("sV", em.now("sV"))
            prev_bc = ("sV", em.now("sV"))
        nv_btiles = em.now("sV")

        g_dma(Wo.rearrange("p (ic o) -> p ic o", ic=8),
              dr["W_o"].rearrange("(ic p) o -> p ic o", p=128),
              wait=[("sS", ns_attn)])
        n_ldg_wo = em.now("ldg")
        for ic in range(8):
            em.emit("vector", lambda v, ic=ic: v.tensor_tensor(
                out=Wo[:, ic * D:(ic + 1) * D], in0=Wo[:, ic * D:(ic + 1) * D],
                in1=gab, op=ALU.mult),
                wait=([("ldg", n_ldg_wo), ("sV", nv_btiles)] if ic == 0 else ()),
                inc=([("sV", 1)] if ic == 7 else []))
        nv_wo = em.now("sV")

        wq_red = []
        for ch in range(8):
            wt = [("sV", nv_attn)]
            if ch >= 2:
                wt.append(("sV", wq_red[ch - 2]))
            qsem = "ldqa" if ch % 2 == 0 else "ldqb"
            em.emit("sync", lambda g, ch=ch: g.dma_start(
                out=wqring[ch % 2], in_=dr["wq"][ch * 128:(ch + 1) * 128, :]),
                wait=wt, inc=[(qsem, 16)])
            nld = em.now(qsem)
            em.emit("vector", lambda v, ch=ch: v.tensor_reduce(
                out=wsum[:, ch:ch + 1], in_=wqring[ch % 2], axis=AX.X, op=ALU.add,
                apply_absolute_value=True), wait=[(qsem, nld)],
                inc=[("sV", 1)])
            wq_red.append(em.now("sV"))
        em.emit("vector", lambda v: v.tensor_reduce(
            out=wsum[:, 8:9], in_=wsum[:, 0:8], axis=AX.X, op=ALU.add),
            wait=[("sV", wq_red[7])], inc=[("sV", 1)])
        nv_wsum = em.now("sV")
        em.emit("tensor", lambda t: t.matmul(OC1[0:1, 0:1], wsum[:, 8:9], onescol,
                start=True, stop=True),
                wait=[("sV", nv_wsum), ("sV", nv_onescol)], inc=[("sT", 1)])
        em.emit("scalar", lambda s: s.copy(out=w1, in_=OC1[0:1, 0:1]),
                wait=[("sT", em.now("sT"))], inc=[("sS", 1)])
        em.emit("tensor", lambda t: t.matmul(OC2[:, 0:1], onesrow, w1,
                start=True, stop=True, skip_group_check=True),
                wait=[("sS", em.now("sS")), oc2_rel[0]], inc=[("sT", 1)])
        em.emit("vector", lambda v: v.tensor_scalar(
            out=wscol, in0=OC2[:, 0:1], scalar1=1.0 / (D * D), scalar2=EPS_FFN,
            op0=ALU.mult, op1=ALU.add), wait=[("sT", em.now("sT"))], inc=[("sD", 1)])
        em.emit("vector", lambda v: v.reciprocal(out=wrecip, in_=wscol),
                wait=[vdrain()])
        em.emit("vector", lambda v: v.tensor_scalar(
            out=wsfin, in0=wscol, scalar1=1.0 / 127.0, scalar2=None, op0=ALU.mult),
            inc=[("sV", 1)])
        nv_wsc = em.now("sV")

        ntr_wq_prev = 0
        for ch in range(8):
            qsem = "ldqa" if ch % 2 == 0 else "ldqb"
            em.emit("sync", lambda g, ch=ch: g.dma_start(
                out=wqring[ch % 2], in_=dr["wq"][ch * 128:(ch + 1) * 128, :]),
                wait=[("sV", nv_wsc if ch < 2 else em.now("sV"))], inc=[(qsem, 16)])
            nld = em.now(qsem)
            em.emit("vector", lambda v, ch=ch: v.tensor_scalar(
                out=rnd_a, in0=wqring[ch % 2], scalar1=wrecip, scalar2=None,
                op0=ALU.mult), wait=[(qsem, nld), ("sV", nv_wsc)] +
                ([("sT", ntr_wq_prev)] if ch else []))
            em.emit("vector", lambda v: v.tensor_scalar(
                out=rnd_b, in0=rnd_a, scalar1=C32, scalar2=-C32, op0=ALU.add, op1=ALU.add))
            em.emit("vector", lambda v: v.tensor_scalar(
                out=rnd_a, in0=rnd_a, scalar1=C64, scalar2=-C64, op0=ALU.add, op1=ALU.add))
            em.emit("vector", lambda v: v.tensor_tensor(
                out=rnd_a, in0=rnd_a, in1=rnd_b, op=ALU.subtract))
            em.emit("vector", lambda v: v.scalar_tensor_tensor(
                out=rnd_a, in0=rnd_a, scalar=pcol, in1=rnd_b, op0=ALU.mult, op1=ALU.add))
            em.emit("vector", lambda v: v.tensor_scalar(
                out=wqbf, in0=rnd_a, scalar1=-1.0, scalar2=1.0,
                op0=ALU.max, op1=ALU.min), inc=[("sV", 1)])
            nv_q = em.now("sV")
            for ic in range(8):
                wt2 = [("sV", nv_q), pt_rel[0]]
                em.emit("tensor", lambda t, ic=ic: t.transpose(
                    PT[:, ic * 128:(ic + 1) * 128], wqbf[:, ic * 128:(ic + 1) * 128], ident),
                    wait=(wt2 if ic == 0 else ()), inc=([("sT", 1)] if ic == 7 else []))
            ntr = em.now("sT")
            ntr_wq_prev = ntr
            em.emit("scalar", lambda s, ch=ch: s.copy(
                out=wqT.rearrange("p (ic o) -> p ic o", o=D)[:, :, ch * 128:(ch + 1) * 128],
                in_=PT.rearrange("p (ic f) -> p ic f", f=128)),
                wait=[("sT", ntr)], inc=[("sS", 1)])
            pt_rel[0] = ("sS", em.now("sS"))
        for ic in range(8):
            em.emit("vector", lambda v, ic=ic: v.tensor_tensor(
                out=wqT[:, ic * D:(ic + 1) * D], in0=wqT[:, ic * D:(ic + 1) * D],
                in1=gbb, op=ALU.mult),
                wait=([pt_rel[0], ("sV", nv_btiles)] if ic == 0 else ()),
                inc=([("sV", 1)] if ic == 7 else []))
        nv_wqT = em.now("sV")
        cutpoint("p4pre")

        mm_psl = [SA[:, 0:512], SA[:, 512:1024], SB[:, 0:512], SB[:, 512:1024]]
        mm_rel = [None, None, None, None]
        mm_i = [0]

        def next_mm(extra):
            sl = mm_i[0] % 4
            mm_i[0] += 1
            wt = list(extra)
            if mm_rel[sl] is not None:
                wt.append(mm_rel[sl])
            return sl, mm_psl[sl], wt

        ns_rstd2_prev = 0
        nv_mv2_prev = 0
        ns_sq2_prev = 0
        ntr_aq_prev = 0
        atr_rel = [None, None]
        aqr_rel = [None, None]
        ot_rel = [None, None]
        x1_done = []
        for qt in range(NT):
            wtx = []
            if qt >= 2:
                wtx.append(("sV", x1_done[qt - 2]))
            xsem = "ldxa" if qt % 2 == 0 else "ldxb"
            em.emit("sync", lambda g, qt=qt: g.dma_start(
                out=xin[qt % 2], in_=dr["x"][qt * 128:(qt + 1) * 128, :]),
                wait=wtx, inc=[(xsem, 16)])
            nldx = em.now(xsem)
            wt = [("sV", nv_attn), pt_rel[0]]
            for fc in range(8):
                em.emit("tensor", lambda t, fc=fc, qt=qt: t.transpose(
                    PT[:, fc * 128:(fc + 1) * 128], attn[qt][:, fc * 128:(fc + 1) * 128], ident),
                    wait=(wt if fc == 0 else ()), inc=([("sT", 1)] if fc == 7 else []))
            ntr = em.now("sT")
            wt_atr = [("sT", ntr)]
            if atr_rel[qt % 2] is not None:
                wt_atr.append(atr_rel[qt % 2])
            em.emit("scalar", lambda s, qt=qt: s.copy(out=attnTr[qt % 2], in_=PT),
                    wait=wt_atr, inc=[("sS", 1)])
            pt_rel[0] = ("sS", em.now("sS"))
            n_att = em.now("sS")
            for oc in range(2):
                sl, pp, wt2 = next_mm([("sS", n_att), ("sV", nv_wo)])
                for ic in range(8):
                    em.emit("tensor", lambda t, ic=ic, pp=pp, qt=qt, oc=oc: t.matmul(
                        pp, attnTr[qt % 2][:, ic * 128:(ic + 1) * 128],
                        Wo[:, ic * D + oc * 512: ic * D + (oc + 1) * 512],
                        start=(ic == 0), stop=(ic == 7), skip_group_check=True),
                        wait=(wt2 if ic == 0 else ()), inc=([("sT", 1)] if ic == 7 else []))
                nmm = em.now("sT")
                em.emit("vector", lambda v, pp=pp, qt=qt, oc=oc: v.tensor_tensor(
                    out=x1[qt][:, oc * 512:(oc + 1) * 512], in0=pp,
                    in1=xin[qt % 2][:, oc * 512:(oc + 1) * 512], op=ALU.add),
                    wait=[("sT", nmm), (xsem, nldx)], inc=[("sV", 1)])
                mm_rel[sl] = ("sV", em.now("sV"))
            atr_rel[qt % 2] = mm_rel[(2 * qt + 1) % 4]
            x1_done.append(em.now("sV"))
            em.emit("vector", lambda v, qt=qt: v.tensor_tensor(
                out=sqbuf, in0=x1[qt], in1=x1[qt], op=ALU.mult),
                wait=[("sV", x1_done[-1])], inc=[("sV", 1)])
            em.emit("vector", lambda v: v.tensor_reduce(
                out=stat[:, 1:2], in_=sqbuf, axis=AX.X, op=ALU.add))
            em.emit("vector", lambda v, qt=qt: v.tensor_reduce(
                out=stat[:, 0:1], in_=x1[qt], axis=AX.X, op=ALU.add),
                inc=[("sD", 1)])
            em.emit("vector", lambda v: v.tensor_scalar(
                out=mv[:, 0:1], in0=stat[:, 0:1], scalar1=1.0 / D, scalar2=None,
                op0=ALU.mult), wait=[vdrain()], inc=[("sD", 1)])
            em.emit("vector", lambda v: v.tensor_tensor(
                out=stat[:, 2:3], in0=mv[:, 0:1], in1=mv[:, 0:1], op=ALU.mult),
                wait=[vdrain()], inc=[("sD", 1)])
            em.emit("vector", lambda v: v.scalar_tensor_tensor(
                out=mv[:, 1:2], in0=stat[:, 1:2], scalar=1.0 / D, in1=stat[:, 2:3],
                op0=ALU.mult, op1=ALU.subtract), wait=[vdrain()], inc=[("sV", 1)])
            nv_mv = em.now("sV")
            nv_mv2_prev = nv_mv
            em.emit("scalar", lambda s: s.activation(out=rstd, in_=mv[:, 1:2],
                    func=AF.Ln, bias=epscol), wait=[("sV", nv_mv), ("sV", nv_onescol)])
            em.emit("scalar", lambda s: s.activation(out=rstd, in_=rstd,
                    func=AF.Exp, scale=-0.5), inc=[("sS", 1)])
            ns_rstd = em.now("sS")
            ns_rstd2_prev = ns_rstd
            em.emit("vector", lambda v: v.scalar_tensor_tensor(
                out=m2, in0=mv[:, 0:1], scalar=-1.0, in1=rstd,
                op0=ALU.mult, op1=ALU.mult), wait=[("sS", ns_rstd)], inc=[("sD", 1)])
            em.emit("vector", lambda v, qt=qt: v.tensor_scalar(
                out=h2a, in0=x1[qt], scalar1=rstd, scalar2=m2, op0=ALU.mult, op1=ALU.add),
                wait=[vdrain()])
            em.emit("vector", lambda v: v.tensor_tensor(out=h2a, in0=h2a, in1=cbb, op=ALU.mult),
                    wait=[("sV", nv_btiles)])
            em.emit("vector", lambda v: v.tensor_tensor(out=h2, in0=h2a, in1=sbb, op=ALU.add))
            em.emit("vector", lambda v, qt=qt: v.tensor_reduce(
                out=ascall[:, qt:qt + 1], in_=h2, axis=AX.X, op=ALU.max,
                apply_absolute_value=True), inc=[("sD", 1)])
            em.emit("vector", lambda v, qt=qt: v.tensor_scalar(
                out=ascall[:, qt:qt + 1], in0=ascall[:, qt:qt + 1], scalar1=EPS_FFN,
                scalar2=None, op0=ALU.add), wait=[vdrain()], inc=[("sD", 1)])
            em.emit("vector", lambda v, qt=qt: v.reciprocal(
                out=rec[:, 0:1], in_=ascall[:, qt:qt + 1]),
                wait=[vdrain()], inc=[("sD", 1)])
            em.emit("vector", lambda v: v.tensor_scalar(
                out=rnd_a, in0=h2, scalar1=rec[:, 0:1], scalar2=127.0,
                op0=ALU.mult, op1=ALU.mult),
                wait=[vdrain()] + ([("sT", ntr_aq_prev)] if qt else []))
            em.emit("vector", lambda v: v.tensor_scalar(
                out=rnd_b, in0=rnd_a, scalar1=C32, scalar2=-C32, op0=ALU.add, op1=ALU.add))
            em.emit("vector", lambda v: v.tensor_scalar(
                out=rnd_a, in0=rnd_a, scalar1=C64, scalar2=-C64, op0=ALU.add, op1=ALU.add))
            em.emit("vector", lambda v: v.tensor_tensor(
                out=rnd_a, in0=rnd_a, in1=rnd_b, op=ALU.subtract))
            em.emit("vector", lambda v: v.scalar_tensor_tensor(
                out=rnd_a, in0=rnd_a, scalar=pcol, in1=rnd_b, op0=ALU.mult, op1=ALU.add))
            em.emit("vector", lambda v: v.tensor_copy(out=aqbf, in_=rnd_a), inc=[("sV", 1)])
            nv_aq = em.now("sV")
            wt3 = [("sV", nv_aq), pt_rel[0]]
            for fc in range(8):
                em.emit("tensor", lambda t, fc=fc: t.transpose(
                    PT[:, fc * 128:(fc + 1) * 128], aqbf[:, fc * 128:(fc + 1) * 128], ident),
                    wait=(wt3 if fc == 0 else ()), inc=([("sT", 1)] if fc == 7 else []))
            ntr = em.now("sT")
            ntr_aq_prev = ntr
            wt_aqr = [("sT", ntr)]
            if aqr_rel[qt % 2] is not None:
                wt_aqr.append(aqr_rel[qt % 2])
            em.emit("scalar", lambda s, qt=qt: s.copy(out=aqTr[qt % 2], in_=PT),
                    wait=wt_aqr, inc=[("sS", 1)])
            pt_rel[0] = ("sS", em.now("sS"))
            n_aqr = em.now("sS")
            em.emit("vector", lambda v, qt=qt: v.tensor_tensor(
                out=scol, in0=ascall[:, qt:qt + 1], in1=wsfin, op=ALU.mult),
                wait=[("sV", nv_wsc)], inc=[("sD", 1)])
            nd_scol = em.now("sD")
            ot = otile[qt % 2]
            for oc in range(2):
                sl, pp, wt2 = next_mm([("sS", n_aqr), ("sV", nv_wqT)])
                for ic in range(8):
                    em.emit("tensor", lambda t, ic=ic, pp=pp, qt=qt, oc=oc: t.matmul(
                        pp, aqTr[qt % 2][:, ic * 128:(ic + 1) * 128],
                        wqT[:, ic * D + oc * 512: ic * D + (oc + 1) * 512],
                        start=(ic == 0), stop=(ic == 7), skip_group_check=True),
                        wait=(wt2 if ic == 0 else ()), inc=([("sT", 1)] if ic == 7 else []))
                nmm = em.now("sT")
                wt4 = [("sT", nmm), ("sD", nd_scol)]
                if ot_rel[qt % 2] is not None and oc == 0:
                    wt4.append(ot_rel[qt % 2])
                em.emit("vector", lambda v, pp=pp, qt=qt, oc=oc, ot=ot: v.scalar_tensor_tensor(
                    out=ot[:, oc * 512:(oc + 1) * 512], in0=pp, scalar=scol,
                    in1=x1[qt][:, oc * 512:(oc + 1) * 512], op0=ALU.mult, op1=ALU.add),
                    wait=wt4, inc=[("sV", 1)])
                mm_rel[sl] = ("sV", em.now("sV"))
            aqr_rel[qt % 2] = mm_rel[(2 * qt + 1) % 4]
            nv_ot = em.now("sV")
            osem = "sta" if qt % 2 == 0 else "stb"
            em.emit("sync", lambda g, qt=qt, ot=ot: g.dma_start(
                out=dr["out"][qt * 128:(qt + 1) * 128, :], in_=ot),
                wait=[("sV", nv_ot)], inc=[(osem, 16)])
            ot_rel[qt % 2] = (osem, em.now(osem))

        if tap("x1_0", [128, D]) is not None:
            em.emit("sync", lambda g: g.dma_start(out=dbg["x1_0"], in_=x1[0]),
                    wait=[("sV", x1_done[0])], inc=[("st", 16)])
        em.wait_only("sync", [("sta", em.now("sta")), ("stb", em.now("stb")),
                              ("st", em.now("st"))])
        sc.pop()
        sc.pop()


    except _Cut:
        em.emit("sync", lambda g: g.dma_start(
            out=dr["out"][0:128, :], in_=xin[0]),
            wait=[(s, em.now(s)) for s in em.cnt], inc=[("sta", 16)])
        em.wait_only("sync", [("sta", em.now("sta"))])

    with nc.Block() as block:
        @block.sync
        def _(e):
            for op in em.q["sync"]:
                op(e)

        @block.gpsimd
        def _(e):
            for op in em.q["gpsimd"]:
                op(e)

        @block.tensor
        def _(e):
            for op in em.q["tensor"]:
                op(e)

        @block.vector
        def _(e):
            for op in em.q["vector"]:
                op(e)

        @block.scalar
        def _(e):
            for op in em.q["scalar"]:
                op(e)

    _cleanup.close()
    return nc, dbg


_CONSTS = _consts()


def kernel(L0_attn_x, L0_attn_W_kv_down, L0_attn_W_k_up, L0_attn_W_v_up,
           L0_attn_W_q, L0_attn_W_q_rope, L0_attn_W_k_rope, L0_attn_W_o,
           L0_ffn_weight, L0_c, L0_wmod, _debug=(), _nc_cache={}):
    x = np.ascontiguousarray(np.asarray(L0_attn_x, np.float32))
    key = tuple(sorted(_debug))
    if key not in _nc_cache:
        _nc_cache[key] = build_program(debug=_debug)
    nc, dbg = _nc_cache[key]
    shared = {
        "wmod": np.ascontiguousarray(np.asarray(L0_wmod, np.float32)),
        "W_kv_down": np.ascontiguousarray(np.asarray(L0_attn_W_kv_down, np.float32)),
        "W_k_up": np.ascontiguousarray(np.asarray(L0_attn_W_k_up, np.float32)),
        "W_v_up": np.ascontiguousarray(np.asarray(L0_attn_W_v_up, np.float32)),
        "W_q": np.ascontiguousarray(np.asarray(L0_attn_W_q, np.float32)),
        "W_q_rope": np.ascontiguousarray(np.asarray(L0_attn_W_q_rope, np.float32)),
        "W_k_rope": np.ascontiguousarray(np.asarray(L0_attn_W_k_rope, np.float32)),
        "W_o": np.ascontiguousarray(np.asarray(L0_attn_W_o, np.float32)),
        "wq": np.ascontiguousarray(np.asarray(L0_ffn_weight, np.float32)),
    }
    shared.update(_CONSTS)
    carr = np.ascontiguousarray(np.asarray(L0_c, np.float32))
    in_maps = []
    for b in range(8):
        m = dict(shared)
        m["x"] = np.ascontiguousarray(x[b])
        m["c"] = np.ascontiguousarray(carr[b])
        in_maps.append(m)
    res = run_bass_kernel_spmd(nc, in_maps, core_ids=list(range(8)))
    out = np.stack([r["out"] for r in res.results], axis=0).astype(np.float32)
    if _debug:
        return out, [dict(r) for r in res.results]
    return out
